# version 01
# baseline (speedup 1.0000x reference)
import numpy as np
from contextlib import ExitStack
import concourse.bass as bass
import concourse.mybir as mybir
from concourse.bass_utils import run_bass_kernel_spmd

F32 = mybir.dt.float32
BF16 = mybir.dt.bfloat16
I32 = mybir.dt.int32
AF = mybir.ActivationFunctionType
ALU = mybir.AluOpType
AX = mybir.AxisListType.X

NCORES = 8
T = 8192
D = 2048
H = 16
DH = 128
TPC = T // NCORES
NEG = -1.0e32
ESHIFT = -32.0
SCALE = float(1.0 / np.sqrt(DH))
LN_EPS = 1e-5


def _rope_tables():
    d = np.arange(0, DH, 2, dtype=np.float64) / DH
    inv_freq = 1.0 / (10000.0 ** d)
    pos = np.arange(T, dtype=np.float64)
    ang = np.outer(inv_freq, pos)
    ang2 = np.concatenate([ang, ang], axis=0)
    return (np.cos(ang2).astype(np.float32), np.sin(ang2).astype(np.float32))


def _r_interleaved():
    R = np.zeros((DH, DH), dtype=np.float32)
    i = np.arange(0, DH, 2)
    R[i, i + 1] = -1.0
    R[i + 1, i] = 1.0
    return np.ascontiguousarray(R.T)


def _r_halfswap():
    R = np.zeros((DH, DH), dtype=np.float32)
    i = np.arange(64)
    R[i, i + 64] = -1.0
    R[i + 64, i] = 1.0
    return np.ascontiguousarray(R.T)


def _l1_block_mask():
    nb = T // 64
    bp = np.arange(nb)
    qb, kb = bp[:, None], bp[None, :]
    dist = qb - kb
    window = (dist >= 0) & (dist <= 1024 // 64)
    glob = kb < max(1, 128 // 64)
    dil = (kb % max(1, 256 // 64)) == 0
    causal = kb <= qb
    return (window | glob | dil) & causal


def _build_plans():
    mask_list = []
    mask_keys = {}

    def mask_id(arr):
        key = arr.tobytes()
        if key not in mask_keys:
            mask_keys[key] = len(mask_list)
            mask_list.append(arr)
        return mask_keys[key]

    causal = np.where(np.arange(128)[:, None] > np.arange(128)[None, :],
                      np.float32(NEG), np.float32(0.0)).astype(np.float32)
    cid = mask_id(causal)
    plan0 = []
    for qs in range(16):
        entries = []
        for j in range(4 * qs + 4):
            u = j - 4 * qs
            if u < 0:
                entries.append((128 * j, 128, 0, 512, []))
            else:
                entries.append((128 * j, 128, 128 * u, 512, [(0, cid)]))
        plan0.append(entries)

    M = _l1_block_mask()
    plan1 = []
    for qs in range(16):
        gl = list(range(4 * qs, 4 * qs + 4))
        entries = []
        for j in range(64):
            stat = []
            for g in gl:
                sub = M[2 * g: 2 * g + 2, 2 * j: 2 * j + 2]
                stat.append(2 if sub.all() else (1 if sub.any() else 0))
            if not any(stat):
                continue
            lo = min(i for i, s in enumerate(stat) if s)
            hi = max(i for i, s in enumerate(stat) if s)
            k0, kw = 128 * j, 128
            ent_masks = []
            span = gl[lo:hi + 1]
            lowall = all(M[2 * g, 2 * j] and M[2 * g + 1, 2 * j] for g in span)
            highdead = not any(M[2 * g, 2 * j + 1] or M[2 * g + 1, 2 * j + 1]
                               for g in span)
            if lowall and highdead:
                kw = 64
            else:
                for i in range(lo, hi + 1):
                    g = gl[i]
                    sub = M[2 * g: 2 * g + 2, 2 * j: 2 * j + 2]
                    if sub.all():
                        continue
                    m = np.where(np.repeat(np.repeat(sub, 64, 0), 64, 1),
                                 np.float32(0.0), np.float32(NEG))
                    m = np.ascontiguousarray(m.T.astype(np.float32))
                    ent_masks.append((128 * (i - lo), mask_id(m)))
            entries.append((k0, kw, 128 * lo, 128 * (hi + 1), ent_masks))
        plan1.append(entries)

    return plan0, plan1, np.stack(mask_list, axis=0)


_PLAN0, _PLAN1, _MASKS = _build_plans()
_NM = _MASKS.shape[0]



ENGS = ("sync", "vector", "scalar", "tensor", "gpsimd")


class Em:

    AUTOSER = ("vector", "scalar", "gpsimd")

    def __init__(self, nc, ctx):
        self.nc = nc
        self.ctx = ctx
        self.q = {e: [] for e in ENGS}
        self.sems = {}
        self.cnt = {}
        self.last = {e: None for e in ENGS}
        for e in self.AUTOSER:
            self.sem("z" + e[0].upper())

    def sem(self, *names):
        for name in names:
            self.sems[name] = self.ctx.enter_context(self.nc.semaphore(name))
            self.cnt[name] = 0

    def op(self, eng, fn, inc=None, k=1, dma=False):
        ser = eng in self.AUTOSER and not dma
        if ser:
            if self.last[eng] is not None:
                pname, pcount = self.last[eng]
                self.wait(eng, pname, pcount)
            if inc is None:
                inc = "z" + eng[0].upper()
        if inc is not None:
            self.cnt[inc] += k
            sem = self.sems[inc]
            if ser:
                self.last[eng] = (inc, self.cnt[inc])
            self.q[eng].append(lambda e, fn=fn, sem=sem, k=k: fn(e).then_inc(sem, k))
        else:
            self.q[eng].append(fn)

    def wait(self, eng, name, val=None):
        v = self.cnt[name] if val is None else val
        if v <= 0:
            return
        sem = self.sems[name]
        self.q[eng].append(lambda e, sem=sem, v=v: e.wait_ge(sem, v))

    def val(self, name):
        return self.cnt[name]

    def barrier(self, engs, names):
        for eng in engs:
            for s in names:
                self.wait(eng, s)

    def flush(self, block):
        @block.sync
        def _(e):
            for f in self.q["sync"]:
                f(e)

        @block.vector
        def _(e):
            for f in self.q["vector"]:
                f(e)

        @block.scalar
        def _(e):
            for f in self.q["scalar"]:
                f(e)

        @block.tensor
        def _(e):
            for f in self.q["tensor"]:
                f(e)

        @block.gpsimd
        def _(e):
            for f in self.q["gpsimd"]:
                f(e)


class Dma2:

    def __init__(self, em, name):
        self.em = em
        self.names = (name + "0", name + "1")
        em.sem(*self.names)

    def issue(self, eng, par, fns):
        name = self.names[par % 2]
        self.em.wait(eng, name)
        for fn in fns:
            self.em.op(eng, fn, inc=name, k=16, dma=True)
        return self.em.val(name)

    def count(self, par):
        return self.em.val(self.names[par % 2])

    def name(self, par):
        return self.names[par % 2]


def emit_rne(em, prod_ap_fn, out_ap_fn, w, i32t, dt_, gt_, inc=None):
    em.op("vector", lambda e: e.tensor_copy(out=i32t[:, 0:w],
                                            in_=prod_ap_fn()))
    em.op("vector", lambda e: e.tensor_tensor(
        out=dt_[:, 0:w], in0=prod_ap_fn(), in1=i32t[:, 0:w],
        op=ALU.subtract))
    em.op("vector", lambda e: e.scalar_tensor_tensor(
        out=gt_[:, 0:w], in0=dt_[:, 0:w], scalar=0.5, in1=i32t[:, 0:w],
        op0=ALU.is_ge, op1=ALU.add))
    em.op("vector", lambda e: e.tensor_scalar(
        out=dt_[:, 0:w], in0=dt_[:, 0:w], scalar1=-0.5, scalar2=None,
        op0=ALU.is_le))
    em.op("vector", lambda e: e.tensor_tensor(
        out=out_ap_fn(), in0=gt_[:, 0:w], in1=dt_[:, 0:w],
        op=ALU.subtract), inc=inc)



def build_attn(plan, debug=False):
    nc = bass.Bass("TRN2", target_bir_lowering=False, detect_race_conditions=True)
    xT = nc.dram_tensor("xT", [D, T], F32, kind="ExternalInput").ap()
    wqT = nc.dram_tensor("wqT", [D, 2 * DH], F32, kind="ExternalInput").ap()
    wkT = nc.dram_tensor("wkT", [D, 2 * DH], F32, kind="ExternalInput").ap()
    wvT = nc.dram_tensor("wvT", [D, 2 * DH], F32, kind="ExternalInput").ap()
    cosT = nc.dram_tensor("cosT", [DH, T], F32, kind="ExternalInput").ap()
    sinT = nc.dram_tensor("sinT", [DH, T], F32, kind="ExternalInput").ap()
    rT = nc.dram_tensor("rT", [DH, DH], F32, kind="ExternalInput").ap()
    identT = nc.dram_tensor("identT", [128, 128], F32, kind="ExternalInput").ap()
    maskc = nc.dram_tensor("maskc", [_NM, 128, 128], F32, kind="ExternalInput").ap()
    ctxT = nc.dram_tensor("ctxT", [2, DH, T], F32, kind="ExternalOutput").ap()

    ikind = "ExternalOutput" if debug else "Internal"
    qT_i = nc.dram_tensor("qT_i", [2, DH, T], BF16, kind=ikind).ap()
    kT_i = nc.dram_tensor("kT_i", [2, DH, T], BF16, kind=ikind).ap()
    v_i = nc.dram_tensor("v_i", [2, T, DH], BF16, kind=ikind).ap()

    TS = 1024
    NSLAB = T // TS
    WQK = {"q": wqT, "k": wkT, "v": wvT}

    with ExitStack() as ctx:
        sb = lambda n, s, d: ctx.enter_context(nc.sbuf_tensor(n, s, d))
        ps = lambda n, s, d=F32: ctx.enter_context(nc.psum_tensor(n, s, d))
        em = Em(nc, ctx)
        em.sem("gld", "sld", "cst", "m1", "ev", "ro", "cb", "vm",
               "vev", "stq", "stv", "kvq", "vms", "s_st", "s_mk", "s_ex",
               "s_pve", "s_ev", "s_tr", "s_stg")
        xld = Dma2(em, "xld")
        swr = Dma2(em, "swr")

        w_s = {k: sb(f"w_{k}", [128, 16 * 256], BF16) for k in ("q", "k", "v")}
        rt_s = sb("rt_s", [128, 128], BF16)
        id_s = sb("id_s", [128, 128], BF16)
        mk_s = sb("mk_s", [128, _NM * 128], F32)
        b32_s = sb("b32_s", [128, 1], F32)
        cos_s = sb("cos_s", [128, TS], BF16)
        sin_s = sb("sin_s", [128, TS], BF16)
        xbf = [sb(f"xbf{i}", [128, 16 * TS], BF16) for i in range(2)]
        qraw = [sb(f"qraw{i}", [128, 512], BF16) for i in range(2)]
        t1_s = sb("t1_s", [128, 512], BF16)
        t2_s = sb("t2_s", [128, 512], BF16)
        qkstage = [sb(f"qkstage{i}", [128, TS], BF16) for i in range(2)]
        vstage = [sb(f"vstage{i}", [128, TS], BF16) for i in range(2)]

        K_s = sb("K_s", [128, T], BF16)
        Q_s = sb("Q_s", [128, T], BF16)
        V_s = sb("V_s", [128, 64 * 129], BF16)
        PT_s = [sb(f"PT{i}", [128, 512], BF16) for i in range(2)]
        ctxn = [sb(f"ctxn{i}", [128, 128], BF16) for i in range(2)]
        rl_s = sb("rl_s", [128, 4], F32)
        stage = [sb(f"stage{i}", [128, 512], F32) for i in range(2)]

        ph1 = ExitStack()
        ps1 = lambda n, s, d=F32: ph1.enter_context(nc.psum_tensor(n, s, d))
        p_qk = [ps1(f"p_qk{i}", [128, 512]) for i in range(2)]
        p_rot = [ps1(f"p_rot{i}", [128, 512]) for i in range(2)]
        p_v = [ps1(f"p_v{i}", [128, 128]) for i in range(2)]
        block = ctx.enter_context(nc.Block())

        for k in ("q", "k", "v"):
            em.op("gpsimd", lambda e, k=k: e.dma_start(
                out=w_s[k][:, :].rearrange("p (c o) -> p c o", o=256),
                in_=WQK[k].rearrange("(c p) o -> p c o", p=128)),
                inc="gld", k=16, dma=True)
        em.op("gpsimd", lambda e: e.dma_start(out=rt_s[:, :], in_=rT[:, :]),
              inc="gld", k=16, dma=True)
        em.op("gpsimd", lambda e: e.dma_start(out=id_s[:, :], in_=identT[:, :]),
              inc="gld", k=16, dma=True)
        em.op("sync", lambda e: e.dma_start(
            out=mk_s[:, :].rearrange("p (m w) -> p m w", w=128),
            in_=maskc.rearrange("m p w -> p m w")), inc="sld", k=16, dma=True)
        em.op("vector", lambda e: e.memset(b32_s[:, :], ESHIFT), inc="cst")

        ui = 0
        vi = 0
        pending_rot = []
        slab_end_m1 = []
        slab_end_vm = []
        slab_end_cb = []
        qk_dma = 0
        v_dma = 0

        def emit_rot(u):
            em.wait("tensor", "ev", u + 1)
            if u >= 2:
                em.wait("tensor", "cb", u - 1)
            em.op("tensor", lambda e, u=u: e.matmul(
                p_rot[u % 2][:, :], rt_s[:, :], qraw[u % 2][:, :],
                start=True, stop=True), inc="ro")

        for sl in range(NSLAB):
            par = sl % 2
            t0 = sl * TS
            if sl >= 2:
                em.wait("gpsimd", "m1", slab_end_m1[sl - 2])
                em.wait("gpsimd", "vm", slab_end_vm[sl - 2])
            if sl >= 1:
                em.wait("gpsimd", "cb", slab_end_cb[sl - 1])
            xcnt = xld.issue("gpsimd", sl, [
                lambda e, par=par, t0=t0: e.dma_start(
                    out=xbf[par][:, :].rearrange("p (c t) -> p c t", t=TS),
                    in_=xT.rearrange("(c p) t -> p c t", p=128)
                    [:, :, t0:t0 + TS]),
                lambda e, t0=t0: e.dma_start(
                    out=cos_s[:, :], in_=cosT[:, t0:t0 + TS]),
                lambda e, t0=t0: e.dma_start(
                    out=sin_s[:, :], in_=sinT[:, t0:t0 + TS]),
            ])
            em.wait("tensor", xld.name(sl), xcnt)
            em.wait("tensor", "gld")
            em.wait("vector", xld.name(sl), xcnt)
            for h in range(2):
                for qk in ("q", "k"):
                    g2 = 4 * sl + 2 * h + (0 if qk == "q" else 1)
                    dst = qT_i if qk == "q" else kT_i
                    for ss in range(TS // 512):
                        u = ui
                        ui += 1
                        if u >= 2:
                            em.wait("tensor", "ev", u - 1)
                        for kc in range(16):
                            em.op("tensor",
                                  lambda e, u=u, h=h, qk=qk, ss=ss, kc=kc,
                                  par=par: e.matmul(
                                      p_qk[u % 2][:, :],
                                      w_s[qk][:, kc * 256 + h * 128:
                                              kc * 256 + h * 128 + 128],
                                      xbf[par][:, kc * TS + ss * 512:
                                               kc * TS + ss * 512 + 512],
                                      start=(kc == 0), stop=(kc == 15)),
                                  inc=("m1" if kc == 15 else None))
                        if pending_rot:
                            emit_rot(pending_rot.pop(0))
                        pending_rot.append(u)
                        em.wait("scalar", "m1", u + 1)
                        if u >= 2:
                            em.wait("scalar", "cb", u - 1)
                        em.op("scalar", lambda e, u=u: e.activation(
                            out=qraw[u % 2][:, :], in_=p_qk[u % 2][:, :],
                            func=AF.Copy), inc="ev")
                        em.wait("vector", "ro", u + 1)
                        if g2 >= 2:
                            em.wait("vector", "stq", 16 * (qk_dma - 1))
                        em.op("vector", lambda e, u=u, ss=ss: e.tensor_tensor(
                            out=t1_s[:, :], in0=qraw[u % 2][:, :],
                            in1=cos_s[:, ss * 512: ss * 512 + 512],
                            op=ALU.mult))
                        em.op("vector", lambda e, u=u, ss=ss: e.tensor_tensor(
                            out=t2_s[:, :], in0=p_rot[u % 2][:, :],
                            in1=sin_s[:, ss * 512: ss * 512 + 512],
                            op=ALU.mult))
                        em.op("vector", lambda e, g2=g2, ss=ss: e.tensor_tensor(
                            out=qkstage[g2 % 2][:, ss * 512: ss * 512 + 512],
                            in0=t1_s[:, :], in1=t2_s[:, :], op=ALU.add),
                            inc="cb")
                    em.wait("sync", "cb")
                    em.wait("sync", "stq")
                    qk_dma += 1
                    em.op("sync", lambda e, dst=dst, h=h, t0=t0, g2=g2:
                          e.dma_start(out=dst[h, :, t0:t0 + TS],
                                      in_=qkstage[g2 % 2][:, :]),
                          inc="stq", k=16, dma=True)
                for tt in range(TS // 128):
                    u = vi
                    vi += 1
                    if u >= 2:
                        em.wait("tensor", "vev", u - 1)
                    for kc in range(16):
                        em.op("tensor", lambda e, u=u, h=h, tt=tt, kc=kc,
                              par=par: e.matmul(
                                  p_v[u % 2][:, :],
                                  xbf[par][:, kc * TS + tt * 128:
                                           kc * TS + tt * 128 + 128],
                                  w_s["v"][:, kc * 256 + h * 128:
                                           kc * 256 + h * 128 + 128],
                                  start=(kc == 0), stop=(kc == 15)),
                              inc=("vm" if kc == 15 else None))
                    em.wait("scalar", "vm", u + 1)
                    if tt == 0 and (2 * sl + h) >= 2:
                        em.wait("scalar", "stv", 16 * (v_dma - 1))
                    em.op("scalar", lambda e, u=u, tt=tt, h=h, sl=sl:
                          e.activation(out=vstage[(2 * sl + h) % 2]
                                       [:, tt * 128: tt * 128 + 128],
                                       in_=p_v[u % 2][:, :], func=AF.Copy),
                          inc="vev")
                em.wait("sync", "vev")
                em.wait("sync", "stv")
                v_dma += 1
                em.op("sync", lambda e, h=h, t0=t0, sl=sl: e.dma_start(
                    out=v_i[h, t0:t0 + TS, :].rearrange(
                        "(tt p) d -> p tt d", p=128),
                    in_=vstage[(2 * sl + h) % 2][:, :].rearrange(
                        "p (tt d) -> p tt d", d=128)), inc="stv", k=16,
                    dma=True)
            while pending_rot:
                emit_rot(pending_rot.pop(0))
            slab_end_m1.append(em.val("m1"))
            slab_end_vm.append(em.val("vm"))
            slab_end_cb.append(em.val("cb"))

        em.barrier(list(ENGS), ["stq", "stv"])

        ph1.close()
        p_ctx = [ps(f"p_ctx{i}", [128, 129]) for i in range(4)]
        p_S = [ps(f"p_S{i}", [128, 512]) for i in range(2)]
        p_T2 = [ps(f"p_T2{i}", [128, 128], BF16) for i in range(2)]

        ei = 0
        tri = 0
        dmai = 0
        for h in range(2):
            if h > 0:
                em.barrier(["sync"], ["s_st", "s_pve"])
                em.barrier(["vector"], ["s_pve"])
            em.wait("sync", "kvq")
            em.op("sync", lambda e, h=h: e.dma_start(
                out=K_s[:, :], in_=kT_i[h, :, :]), inc="kvq", k=16, dma=True)
            em.op("sync", lambda e, h=h: e.dma_start(
                out=Q_s[:, :], in_=qT_i[h, :, :]), inc="kvq", k=16, dma=True)
            em.op("vector", lambda e: e.memset(V_s[:, :], 1.0), inc="vms")
            em.wait("sync", "vms")
            em.op("sync", lambda e, h=h: e.dma_start(
                out=V_s[:, :].rearrange("p (kt w) -> p kt w", w=129)
                [:, :, 0:128],
                in_=v_i[h, :, :].rearrange("(kt p) d -> p kt d", p=128)),
                inc="kvq", k=16, dma=True)
            em.barrier(["tensor", "scalar", "vector"], ["kvq", "vms"])
            em.barrier(["scalar"], ["cst"])
            em.barrier(["vector"], ["sld", "cst"])

            for qs in range(16):
                entries = plan[qs]
                cov = {u: [i for i, (k0, kw, q0, q1, m) in enumerate(entries)
                           if q0 // 128 <= u < q1 // 128] for u in range(4)}
                for u in range(4):
                    assert cov[u], (qs, u)
                for i, (k0, kw, q0, q1, mks) in enumerate(entries):
                    pp = ei % 2
                    cols = q1 - q0
                    if ei >= 2:
                        em.wait("tensor", "s_ex", ei - 1)
                    em.op("tensor", lambda e, pp=pp, k0=k0, kw=kw, qs=qs,
                          q0=q0, cols=cols: e.matmul(
                              p_S[pp][0:kw, 0:cols], K_s[:, k0:k0 + kw],
                              Q_s[:, qs * 512 + q0: qs * 512 + q0 + cols],
                              start=True, stop=True), inc="s_st")
                    for (off, mid) in mks:
                        em.wait("vector", "s_st")
                        em.op("vector", lambda e, pp=pp, kw=kw, off=off,
                              mid=mid: e.tensor_tensor(
                                  out=p_S[pp][0:kw, off:off + 128],
                                  in0=p_S[pp][0:kw, off:off + 128],
                                  in1=mk_s[0:kw, mid * 128: mid * 128 + 128],
                                  op=ALU.add), inc="s_mk")
                    em.wait("scalar", "s_st")
                    em.wait("scalar", "s_mk")
                    if ei >= 2:
                        em.wait("scalar", "s_pve", ei - 1)
                    em.op("scalar", lambda e, pp=pp, kw=kw, cols=cols:
                          e.activation(out=PT_s[pp][0:kw, 0:cols],
                                       in_=p_S[pp][0:kw, 0:cols], func=AF.Exp,
                                       bias=b32_s[0:kw, 0:1], scale=SCALE),
                          inc="s_ex")
                    em.wait("tensor", "s_ex")
                    ulist = [u for u in range(4) if q0 // 128 <= u < q1 // 128]
                    for n_, u in enumerate(ulist):
                        em.op("tensor", lambda e, pp=pp, kw=kw, u=u, q0=q0,
                              k0=k0, fi=(cov[u][0] == i), la=(cov[u][-1] == i):
                              e.matmul(
                                  p_ctx[u][:, :],
                                  PT_s[pp][0:kw,
                                           u * 128 - q0: u * 128 - q0 + 128],
                                  V_s[0:kw, (k0 // 128) * 129:
                                      (k0 // 128) * 129 + 129],
                                  start=fi, stop=la),
                              inc=("s_pve" if n_ == len(ulist) - 1 else None))
                    ei += 1
                em.wait("vector", "s_pve")
                if dmai >= 2:
                    em.wait("vector", swr.name(dmai), 16 * (dmai // 2))
                for u in range(4):
                    em.op("vector", lambda e, u=u: e.reciprocal(
                        out=rl_s[:, u:u + 1], in_=p_ctx[u][:, 128:129]))
                    if tri + u >= 2:
                        em.wait("vector", "s_tr", tri + u - 1)
                    em.op("vector", lambda e, u=u: e.tensor_scalar(
                        out=ctxn[u % 2][:, :], in0=p_ctx[u][:, 0:128],
                        scalar1=rl_s[:, u:u + 1], scalar2=None, op0=ALU.mult),
                        inc="s_ev")
                    em.wait("tensor", "s_ev")
                    if tri + u >= 2:
                        em.wait("tensor", "s_stg", tri + u - 1)
                    em.op("tensor", lambda e, u=u: e.transpose(
                        p_T2[u % 2][:, :], ctxn[u % 2][:, :], id_s[:, :]),
                        inc="s_tr")
                    em.wait("vector", "s_tr")
                    em.op("vector", lambda e, qs=qs, u=u: e.tensor_copy(
                        out=stage[qs % 2][:, u * 128: u * 128 + 128],
                        in_=p_T2[u % 2][:, :]), inc="s_stg")
                tri += 4
                em.wait("sync", "s_stg")
                swr.issue("sync", dmai, [
                    lambda e, h=h, qs=qs: e.dma_start(
                        out=ctxT[h, :, qs * 512: qs * 512 + 512],
                        in_=stage[qs % 2][:, :])])
                dmai += 1

        em.barrier(["gpsimd"], [swr.names[0], swr.names[1]])
        em.flush(block)
    return nc



def alloc_ln(nc, ctx, ntok):
    sb = lambda n, s, d: ctx.enter_context(nc.sbuf_tensor(n, s, d))
    st = {
        "xb": sb("ln_xb", [128, 16 * 128], BF16),
        "sqb": sb("ln_sqb", [128, 16 * 128], BF16),
        "mrow": sb("ln_mrow", [1, ntok], F32),
        "rrow": sb("ln_rrow", [1, ntok], F32),
        "mtile": sb("ln_mtile", [128, 2], F32),
        "vtile": sb("ln_vtile", [128, 2], F32),
        "nout": [sb(f"ln_nout{i}", [128, 512], F32) for i in range(2)],
        "ntmp": sb("ln_ntmp", [128, 512], F32),
        "onesb": sb("ln_onesb", [128, 1], BF16),
        "onesc": sb("ln_onesc", [1, 128], F32),
        "eps": sb("ln_eps", [128, 1], F32),
    }
    return st


def emit_ln_init(em, st):
    em.op("gpsimd", lambda e: e.memset(st["onesb"][:, :], 1.0), inc="lni")
    em.op("gpsimd", lambda e: e.memset(st["onesc"][:, :], 1.0), inc="lni")
    em.op("gpsimd", lambda e: e.memset(st["eps"][:, :], LN_EPS), inc="lni")


def emit_ln(em, nc, pre, st, psA, psB, psS, psT, idf_s, out_dram, ntok, pref):
    s_ev, s_mm, s_sc, s_tr, s_rw, s_bc, s_nm, s_nn, s_sq, s_va, s_q2 = (
        pref + x for x in ("ev", "mm", "sc", "tr", "rw", "bc", "nm", "no",
                           "sq", "va", "q2"))
    em.sem(s_ev, s_mm, s_sc, s_tr, s_rw, s_bc, s_nm, s_nn, s_sq, s_va, s_q2)
    wrd = Dma2(em, pref + "wr")
    em.barrier(["vector", "scalar"], ["lni"])
    ntile = ntok // 128
    for tt in range(ntile):
        if tt >= 1:
            em.wait("vector", s_mm)
            em.wait("scalar", s_mm)
        for c in range(16):
            em.op("vector", lambda e, c=c, tt=tt: e.tensor_copy(
                out=st["xb"][:, c * 128:c * 128 + 128],
                in_=pre[:, c * ntok + tt * 128: c * ntok + tt * 128 + 128]),
                inc=(s_ev if c == 15 else None))
            em.op("scalar", lambda e, c=c, tt=tt: e.activation(
                out=st["sqb"][:, c * 128:c * 128 + 128],
                in_=pre[:, c * ntok + tt * 128: c * ntok + tt * 128 + 128],
                func=AF.Square), inc=(s_sq if c == 15 else None))
        em.wait("tensor", s_ev)
        em.wait("tensor", s_sq)
        if tt >= 1:
            em.wait("tensor", s_sc)
        for c in range(16):
            em.op("tensor", lambda e, c=c: e.matmul(
                psS[0][:, 0:1], st["xb"][:, c * 128:c * 128 + 128],
                st["onesb"][:, 0:1], start=(c == 0), stop=(c == 15)))
            em.op("tensor", lambda e, c=c: e.matmul(
                psS[1][:, 0:1], st["sqb"][:, c * 128:c * 128 + 128],
                st["onesb"][:, 0:1], start=(c == 0), stop=(c == 15)),
                inc=(s_mm if c == 15 else None))
        em.wait("vector", s_mm)
        if tt >= 1:
            em.wait("vector", s_tr)
        em.op("vector", lambda e: e.tensor_scalar(
            out=st["mtile"][:, 0:1], in0=psS[0][:, 0:1], scalar1=1.0 / D,
            scalar2=None, op0=ALU.mult))
        em.op("vector", lambda e: e.tensor_scalar(
            out=st["vtile"][:, 0:1], in0=psS[1][:, 0:1], scalar1=1.0 / D,
            scalar2=None, op0=ALU.mult))
        em.op("vector", lambda e: e.tensor_tensor(
            out=st["vtile"][:, 1:2], in0=st["mtile"][:, 0:1],
            in1=st["mtile"][:, 0:1], op=ALU.mult))
        em.op("vector", lambda e: e.tensor_tensor(
            out=st["vtile"][:, 0:1], in0=st["vtile"][:, 0:1],
            in1=st["vtile"][:, 1:2], op=ALU.subtract), inc=s_va)
        em.wait("scalar", s_va)
        em.op("scalar", lambda e: e.activation(
            out=st["vtile"][:, 1:2], in_=st["vtile"][:, 0:1], func=AF.Sqrt,
            bias=st["eps"][:, 0:1], scale=1.0), inc=s_q2)
        em.wait("vector", s_q2)
        em.op("vector", lambda e: e.reciprocal(
            out=st["mtile"][:, 1:2], in_=st["vtile"][:, 1:2]), inc=s_sc)
        em.wait("tensor", s_sc)
        em.op("tensor", lambda e: e.transpose(
            psT[0:1, 0:128], st["mtile"][:, 0:1], idf_s[:, :]))
        em.op("tensor", lambda e: e.transpose(
            psT[0:1, 128:256], st["mtile"][:, 1:2], idf_s[:, :]), inc=s_tr)
        em.wait("vector", s_tr)
        em.op("vector", lambda e, tt=tt: e.tensor_copy(
            out=st["mrow"][0:1, tt * 128: tt * 128 + 128], in_=psT[0:1, 0:128]))
        em.op("vector", lambda e, tt=tt: e.tensor_copy(
            out=st["rrow"][0:1, tt * 128: tt * 128 + 128],
            in_=psT[0:1, 128:256]), inc=s_rw)
    dmac = 0
    for ts in range(ntok // 512):
        em.wait("tensor", s_rw)
        em.wait("tensor", s_sq)
        if ts >= 1:
            em.wait("tensor", s_nm)
        em.op("tensor", lambda e, ts=ts: e.matmul(
            psA[:, :], st["onesc"][0:1, 0:128],
            st["mrow"][0:1, ts * 512: ts * 512 + 512], start=True, stop=True))
        em.op("tensor", lambda e, ts=ts: e.matmul(
            psB[:, :], st["onesc"][0:1, 0:128],
            st["rrow"][0:1, ts * 512: ts * 512 + 512], start=True, stop=True),
            inc=s_bc)
        em.wait("vector", s_bc)
        for c in range(16):
            par = (ts * 16 + c) % 2
            if dmac >= 2:
                em.wait("vector", wrd.name(dmac), 16 * (dmac // 2))
            em.op("vector", lambda e, c=c, ts=ts: e.tensor_tensor(
                out=st["ntmp"][:, :],
                in0=pre[:, c * ntok + ts * 512: c * ntok + ts * 512 + 512],
                in1=psA[:, :], op=ALU.subtract))
            em.op("vector", lambda e, par=par: e.tensor_tensor(
                out=st["nout"][par][:, :],
                in0=st["ntmp"][:, :], in1=psB[:, :], op=ALU.mult), inc=s_nn)
            em.wait("sync", s_nn)
            em.op("sync", lambda e, c=c, ts=ts, par=par: e.nop())
            wrd.issue("sync", dmac, [
                lambda e, c=c, ts=ts, par=par: e.dma_start(
                    out=out_dram[c * 128:(c + 1) * 128,
                                 ts * 512: ts * 512 + 512],
                    in_=st["nout"][par][:, :])])
            dmac += 1
        em.op("vector", lambda e: e.engine_nop(), inc=s_nm)
    return [wrd.names[0], wrd.names[1]]



def build_proj_ln():
    nc = bass.Bass("TRN2", target_bir_lowering=False, detect_race_conditions=True)
    ctxTs = nc.dram_tensor("ctxTs", [D, TPC], F32, kind="ExternalInput").ap()
    woT = nc.dram_tensor("woT", [D, D], F32, kind="ExternalInput").ap()
    resT = nc.dram_tensor("resT", [D, TPC], F32, kind="ExternalInput").ap()
    identT = nc.dram_tensor("identT", [128, 128], F32, kind="ExternalInput").ap()
    outT = nc.dram_tensor("outT", [D, TPC], F32, kind="ExternalOutput").ap()

    with ExitStack() as ctx:
        sb = lambda n, s, d: ctx.enter_context(nc.sbuf_tensor(n, s, d))
        ps = lambda n, s, d=F32: ctx.enter_context(nc.psum_tensor(n, s, d))
        em = Em(nc, ctx)
        em.sem("gld", "sld", "mm", "evc", "lni")
        wld = Dma2(em, "wld")

        ctxb = sb("ctxb", [128, 16 * TPC], BF16)
        res = sb("res", [128, 16 * TPC], F32)
        pre = sb("pre", [128, 16 * TPC], F32)
        idf_s = sb("idf_s", [128, 128], F32)
        wsl = [sb(f"wsl{i}", [128, 16 * 256], BF16) for i in range(2)]
        lnst = alloc_ln(nc, ctx, TPC)

        p_mm = [ps(f"p_mm{i}", [128, 512]) for i in range(2)]
        p_A = ps("p_A", [128, 512])
        p_B = ps("p_B", [128, 512])
        p_S = [ps("p_S0", [128, 1]), ps("p_S1", [128, 1])]
        p_T = ps("p_T", [1, 256])
        block = ctx.enter_context(nc.Block())

        em.op("gpsimd", lambda e: e.dma_start(
            out=ctxb[:, :].rearrange("p (c t) -> p c t", t=TPC),
            in_=ctxTs.rearrange("(c p) t -> p c t", p=128)), inc="gld", k=16, dma=True)
        em.op("sync", lambda e: e.dma_start(
            out=res[:, :].rearrange("p (c t) -> p c t", t=TPC),
            in_=resT.rearrange("(c p) t -> p c t", p=128)), inc="sld", k=16)
        em.op("sync", lambda e: e.dma_start(out=idf_s[:, :], in_=identT[:, :]),
              inc="sld", k=16)
        emit_ln_init(em, lnst)

        mi = 0
        for osl in range(8):
            if osl >= 2:
                em.wait("gpsimd", "mm", 4 * (osl - 1))
            wcnt = wld.issue("gpsimd", osl, [
                lambda e, osl=osl: e.dma_start(
                    out=wsl[osl % 2][:, :].rearrange("p (c o) -> p c o", o=256),
                    in_=woT.rearrange("(c p) o -> p c o", p=128)
                    [:, :, osl * 256: osl * 256 + 256])])
            for oc4 in range(2):
                o = 2 * osl + oc4
                for ts in range(TPC // 512):
                    u = mi
                    mi += 1
                    em.wait("tensor", "gld")
                    em.wait("tensor", "sld")
                    em.wait("tensor", wld.name(osl), wcnt)
                    if u >= 2:
                        em.wait("tensor", "evc", u - 1)
                    for kc in range(16):
                        em.op("tensor", lambda e, u=u, osl=osl, oc4=oc4, ts=ts,
                              kc=kc: e.matmul(
                                  p_mm[u % 2][:, :],
                                  wsl[osl % 2][:, kc * 256 + oc4 * 128:
                                               kc * 256 + oc4 * 128 + 128],
                                  ctxb[:, kc * TPC + ts * 512:
                                       kc * TPC + ts * 512 + 512],
                                  start=(kc == 0), stop=(kc == 15)),
                              inc=("mm" if kc == 15 else None))
                    em.wait("vector", "mm", u + 1)
                    em.op("vector", lambda e, u=u, o=o, ts=ts: e.tensor_tensor(
                        out=pre[:, o * TPC + ts * 512: o * TPC + ts * 512 + 512],
                        in0=p_mm[u % 2][:, :],
                        in1=res[:, o * TPC + ts * 512: o * TPC + ts * 512 + 512],
                        op=ALU.add), inc="evc")
        em.barrier(list(ENGS), ["evc"])
        wrn = emit_ln(em, nc, pre, lnst, p_A, p_B, p_S, p_T, idf_s, outT,
                      TPC, "L")
        em.barrier(["gpsimd"], wrn)
        em.flush(block)
    return nc



def build_ffn0():
    nc = bass.Bass("TRN2", target_bir_lowering=False, detect_race_conditions=True)
    xTs = nc.dram_tensor("xTs", [D, TPC], F32, kind="ExternalInput").ap()
    w1T = nc.dram_tensor("w1T", [D, D], F32, kind="ExternalInput").ap()
    b1 = nc.dram_tensor("b1", [D], F32, kind="ExternalInput").ap()
    w2 = nc.dram_tensor("w2", [D, 8], F32, kind="ExternalInput").ap()
    b2 = nc.dram_tensor("b2", [1, 8], F32, kind="ExternalInput").ap()
    vfT = nc.dram_tensor("vfT", [D, 512], F32, kind="ExternalInput").ap()
    uf = nc.dram_tensor("uf", [512, D], F32, kind="ExternalInput").ap()
    identT = nc.dram_tensor("identT", [128, 128], F32, kind="ExternalInput").ap()
    outT = nc.dram_tensor("outT", [D, TPC], F32, kind="ExternalOutput").ap()

    with ExitStack() as ctx:
        sb = lambda n, s, d: ctx.enter_context(nc.sbuf_tensor(n, s, d))
        ps = lambda n, s, d=F32: ctx.enter_context(nc.psum_tensor(n, s, d))
        em = Em(nc, ctx)
        em.sem("gld", "sld", "cst", "mm", "evc", "lni", "gmm", "gex",
               "gal", "vmm", "vev", "wtr", "wts", "umm", "uev")
        wld = Dma2(em, "wld")

        x1 = sb("x1", [128, 16 * TPC], F32)
        cswap = ExitStack()
        xbf = cswap.enter_context(
            nc.sbuf_tensor("xbf", [128, 16 * TPC], BF16, side="right"))
        hbf = cswap.enter_context(
            nc.sbuf_tensor("hbf", [128, 16 * TPC], BF16, side="right"))
        idb_s = sb("idb_s", [128, 128], BF16)
        idf_s = sb("idf_s", [128, 128], F32)
        wsl = [sb(f"wsl{i}", [128, 16 * 128], BF16) for i in range(2)]
        b1_s = sb("b1_s", [128, 16], F32)
        b2_s = sb("b2_s", [1, 8], BF16)
        onesr = sb("onesr", [1, 128], BF16)
        vfb = sb("vfb", [128, 16 * 512], BF16)
        ufb = sb("ufb", [128, 4 * 2048], BF16)
        ex_s = sb("ex_s", [128, 8], F32)
        asum = sb("asum", [128, 2], F32)
        alpha = sb("alpha", [128, 8 * 8], F32)
        wq_s = sb("wq_s", [128, 512], BF16)
        wT_s = sb("wT_s", [128, 4 * TPC], BF16)
        lnst = alloc_ln(nc, ctx, TPC)

        p_mm = [ps(f"p_mm{i}", [128, 512]) for i in range(2)]
        pgrp = ExitStack()
        p_g = pgrp.enter_context(nc.psum_tensor("p_g", [128, 8], F32))
        p_wt = pgrp.enter_context(nc.psum_tensor("p_wt", [128, 128], BF16))
        block = ctx.enter_context(nc.Block())

        em.op("sync", lambda e: e.dma_start(
            out=x1[:, :].rearrange("p (c t) -> p c t", t=TPC),
            in_=xTs.rearrange("(c p) t -> p c t", p=128)), inc="sld", k=16)
        em.op("gpsimd", lambda e: e.dma_start(
            out=xbf[:, :].rearrange("p (c t) -> p c t", t=TPC),
            in_=xTs.rearrange("(c p) t -> p c t", p=128)), inc="gld", k=16,
            dma=True)
        em.op("gpsimd", lambda e: e.dma_start(
            out=b1_s[:, :], in_=b1.rearrange("(c p) -> p c", p=128),
            allow_slow_non_contiguous=True), inc="gld", k=16, dma=True)
        em.op("gpsimd", lambda e: e.dma_start(out=b2_s[:, :], in_=b2[:, :]),
              inc="gld", k=16, dma=True)
        em.op("gpsimd", lambda e: e.dma_start(
            out=vfb[:, :].rearrange("p (c o) -> p c o", o=512),
            in_=vfT.rearrange("(c p) o -> p c o", p=128)), inc="gld", k=16, dma=True)
        em.op("gpsimd", lambda e: e.dma_start(
            out=ufb[:, :].rearrange("p (c o) -> p c o", o=2048),
            in_=uf.rearrange("(c p) o -> p c o", p=128)), inc="gld", k=16, dma=True)
        em.op("gpsimd", lambda e: e.dma_start(out=idb_s[:, :], in_=identT[:, :]),
              inc="gld", k=16, dma=True)
        em.op("sync", lambda e: e.dma_start(out=idf_s[:, :], in_=identT[:, :]),
              inc="sld", k=16)
        em.op("vector", lambda e: e.memset(onesr[:, :], 1.0), inc="cst")
        emit_ln_init(em, lnst)

        mi = 0
        for osl in range(16):
            if osl >= 2:
                em.wait("gpsimd", "mm", 2 * (osl - 1))
            wcnt = wld.issue("gpsimd", osl, [
                lambda e, osl=osl: e.dma_start(
                    out=wsl[osl % 2][:, :].rearrange("p (c o) -> p c o", o=128),
                    in_=w1T.rearrange("(c p) o -> p c o", p=128)
                    [:, :, osl * 128: osl * 128 + 128])])
            for oc4 in range(1):
                o = osl
                for ts in range(TPC // 512):
                    u = mi
                    mi += 1
                    em.wait("tensor", "gld")
                    em.wait("tensor", "sld")
                    em.wait("tensor", "cst")
                    em.wait("tensor", wld.name(osl), wcnt)
                    if u >= 2:
                        em.wait("tensor", "evc", u - 1)
                    for kc in range(16):
                        em.op("tensor", lambda e, u=u, osl=osl, ts=ts,
                              kc=kc: e.matmul(
                                  p_mm[u % 2][:, :],
                                  wsl[osl % 2][:, kc * 128: kc * 128 + 128],
                                  xbf[:, kc * TPC + ts * 512:
                                      kc * TPC + ts * 512 + 512],
                                  start=(kc == 0), stop=(kc == 15)),
                              inc=("mm" if kc == 15 else None))
                    em.wait("scalar", "mm", u + 1)
                    em.op("scalar", lambda e, u=u, o=o, ts=ts: e.activation(
                        out=hbf[:, o * TPC + ts * 512: o * TPC + ts * 512 + 512],
                        in_=p_mm[u % 2][:, :], func=AF.Relu,
                        bias=b1_s[:, o:o + 1], scale=1.0), inc="evc")
        em.barrier(["tensor"], ["evc"])

        w2b = sb("w2b", [128, 16 * 8], BF16)
        w2cnt = wld.issue("gpsimd", 16, [
            lambda e: e.dma_start(
                out=w2b[:, :].rearrange("p (c o) -> p c o", o=8),
                in_=w2.rearrange("(c p) o -> p c o", p=128))])
        em.wait("tensor", wld.name(16), w2cnt)
        for tt in range(TPC // 128):
            if tt >= 1:
                em.wait("tensor", "gex", tt)
            for kc in range(16):
                em.op("tensor", lambda e, tt=tt, kc=kc: e.matmul(
                    p_g[:, :], hbf[:, kc * TPC + tt * 128: kc * TPC + tt * 128 + 128],
                    w2b[:, kc * 8: kc * 8 + 8], start=(kc == 0), stop=False))
            em.op("tensor", lambda e: e.matmul(
                p_g[:, :], onesr[0:1, :], b2_s[0:1, :], start=False, stop=True),
                inc="gmm")
            em.wait("scalar", "gmm", tt + 1)
            if tt >= 1:
                em.wait("scalar", "gal", tt)
            em.op("scalar", lambda e: e.activation(
                out=ex_s[:, :], in_=p_g[:, :], func=AF.Exp,
                accum_out=asum[:, 0:1]), inc="gex")
            em.wait("vector", "gex", tt + 1)
            em.op("vector", lambda e: e.reciprocal(
                out=asum[:, 1:2], in_=asum[:, 0:1]))
            em.op("vector", lambda e, tt=tt: e.tensor_scalar(
                out=alpha[:, tt * 8: tt * 8 + 8], in0=ex_s[:, :],
                scalar1=asum[:, 1:2], scalar2=None, op0=ALU.mult), inc="gal")
        em.barrier(["tensor"], ["gal"])

        tri = 0
        for tt in range(TPC // 128):
            u = tt
            if u >= 2:
                em.wait("tensor", "vev", u - 1)
            for kc in range(16):
                em.op("tensor", lambda e, u=u, tt=tt, kc=kc: e.matmul(
                    p_mm[u % 2][:, :],
                    xbf[:, kc * TPC + tt * 128: kc * TPC + tt * 128 + 128],
                    vfb[:, kc * 512: kc * 512 + 512],
                    start=(kc == 0), stop=(kc == 15)),
                    inc=("vmm" if kc == 15 else None))
            em.wait("vector", "vmm", u + 1)
            if u >= 1:
                em.wait("vector", "wtr", 4 * u)
            for k8 in range(8):
                em.op("vector", lambda e, u=u, tt=tt, k8=k8: e.tensor_scalar(
                    out=wq_s[:, k8 * 64: k8 * 64 + 64],
                    in0=p_mm[u % 2][:, k8 * 64: k8 * 64 + 64],
                    scalar1=alpha[:, tt * 8 + k8: tt * 8 + k8 + 1],
                    scalar2=None, op0=ALU.mult))
            em.op("vector", lambda e: e.engine_nop(), inc="vev")
            for c in range(4):
                k = tri
                tri += 1
                em.wait("tensor", "vev", u + 1)
                if k >= 1:
                    em.wait("tensor", "wts", k)
                em.op("tensor", lambda e, c=c: e.transpose(
                    p_wt[:, :], wq_s[:, c * 128: c * 128 + 128], idb_s[:, :]),
                    inc="wtr")
                em.wait("vector", "wtr", k + 1)
                em.op("vector", lambda e, c=c, tt=tt: e.tensor_copy(
                    out=wT_s[:, c * TPC + tt * 128: c * TPC + tt * 128 + 128],
                    in_=p_wt[:, :]), inc="wts")
        em.barrier(["tensor"], ["wts"])

        cswap.close()
        pgrp.close()
        pre = ctx.enter_context(
            nc.sbuf_tensor("pre", [128, 16 * TPC], F32, side="right"))
        p_A = ps("p_A", [128, 512])
        p_B = ps("p_B", [128, 512])
        p_S = [ps("p_S0", [128, 1]), ps("p_S1", [128, 1])]
        p_T = ps("p_T", [1, 256])

        mi = 0
        for oc in range(16):
            for ts in range(TPC // 512):
                u = mi
                mi += 1
                if u >= 2:
                    em.wait("tensor", "uev", u - 1)
                for c in range(4):
                    em.op("tensor", lambda e, u=u, oc=oc, ts=ts, c=c: e.matmul(
                        p_mm[u % 2][:, :],
                        ufb[:, c * 2048 + oc * 128: c * 2048 + oc * 128 + 128],
                        wT_s[:, c * TPC + ts * 512: c * TPC + ts * 512 + 512],
                        start=(c == 0), stop=(c == 3)),
                        inc=("umm" if c == 3 else None))
                em.wait("vector", "umm", u + 1)
                em.op("vector", lambda e, u=u, oc=oc, ts=ts: e.tensor_tensor(
                    out=pre[:, oc * TPC + ts * 512: oc * TPC + ts * 512 + 512],
                    in0=p_mm[u % 2][:, :],
                    in1=x1[:, oc * TPC + ts * 512: oc * TPC + ts * 512 + 512],
                    op=ALU.add), inc="uev")
        em.barrier(list(ENGS), ["uev"])
        wrn = emit_ln(em, nc, pre, lnst, p_A, p_B, p_S, p_T, idf_s, outT,
                      TPC, "L")
        em.barrier(["gpsimd"], wrn)
        em.flush(block)
    return nc


def _hadamard(n):
    H = np.array([[1.0]], dtype=np.float32)
    while H.shape[0] < n:
        H = np.block([[H, H], [H, -H]])
    return np.ascontiguousarray(H.astype(np.float32))



def build_wquant():
    nc = bass.Bass("TRN2", target_bir_lowering=False, detect_race_conditions=True)
    wg = nc.dram_tensor("wg", [256, 8192], F32, kind="ExternalInput").ap()
    wu = nc.dram_tensor("wu", [256, 8192], F32, kind="ExternalInput").ap()
    wd = nc.dram_tensor("wd", [1024, 2048], F32, kind="ExternalInput").ap()
    mg = nc.dram_tensor("mg", [256, 8192], F32, kind="ExternalOutput").ap()
    mu = nc.dram_tensor("mu", [256, 8192], F32, kind="ExternalOutput").ap()
    md = nc.dram_tensor("md", [1024, 2048], F32, kind="ExternalOutput").ap()
    sv = nc.dram_tensor("sv", [1, 4], F32, kind="ExternalOutput").ap()
    cc_in = nc.dram_tensor("cc_in", [1, 4], F32, kind="Internal").ap()
    cc_out = nc.dram_tensor("cc_out", [1, 4], F32, kind="Internal").ap()

    NW = float(8192 * 2048)
    mats = [(wg, mg, 2, 4), (wu, mu, 2, 4), (wd, md, 8, 1)]

    with ExitStack() as ctx:
        sb = lambda n, s, d: ctx.enter_context(nc.sbuf_tensor(n, s, d))
        ps = lambda n, s, d=F32: ctx.enter_context(nc.psum_tensor(n, s, d))
        em = Em(nc, ctx)
        em.sem("acc", "par", "ccs", "ccv", "bc", "srow", "mout", "q1")
        wt = [sb(f"wt{i}", [128, 2048], F32) for i in range(2)]
        i32t = sb("i32t", [128, 2048], I32)
        prodt = sb("prodt", [128, 2048], F32)
        rt_ = sb("rt_", [128, 2048], F32)
        dsc = sb("dsc", [128, 2048], F32)
        gsc = sb("gsc", [128, 2048], F32)
        mt = [sb(f"mt{i}", [128, 2048], F32) for i in range(2)]
        accs = sb("accs", [128, 4], F32)
        rowp = sb("rowp", [1, 4], F32)
        srow = sb("srow", [1, 4], F32)
        rsbc = sb("rsbc", [128, 4], F32)
        onesb = sb("onesb", [128, 1], F32)
        onesc = sb("onesc", [1, 128], F32)
        p_dot = ps("p_dot", [1, 4])
        p_bc = ps("p_bc", [128, 4])
        wld = Dma2(em, "wld")
        mst = Dma2(em, "mst")
        block = ctx.enter_context(nc.Block())

        em.op("vector", lambda e: e.memset(accs[:, :], 0.0), inc="q1")
        em.op("vector", lambda e: e.memset(onesb[:, :], 1.0), inc="q1")
        em.op("vector", lambda e: e.memset(onesc[:, :], 1.0), inc="q1")
        em.op("vector", lambda e: e.memset(rowp[:, :], 0.0), inc="q1")

        ti = 0
        for j, (w_in, w_out, nch, nct) in enumerate(mats):
            for c in range(nch):
                for t in range(nct):
                    par = ti % 2
                    if ti >= 2:
                        em.wait("gpsimd", "acc", ti - 1)
                    wld.issue("gpsimd", ti, [
                        lambda e, w_in=w_in, c=c, t=t, par=par: e.dma_start(
                            out=wt[par][:, :],
                            in_=w_in[c * 128:(c + 1) * 128,
                                     t * 2048:(t + 1) * 2048])])
                    em.wait("vector", wld.name(ti), wld.count(ti))
                    em.op("vector", lambda e, par=par: e.tensor_reduce(
                        out=accs[:, 3:4], in_=wt[par][:, :], axis=AX,
                        op=ALU.add, apply_absolute_value=True))
                    em.op("vector", lambda e, j=j: e.tensor_tensor(
                        out=accs[:, j:j + 1], in0=accs[:, j:j + 1],
                        in1=accs[:, 3:4], op=ALU.add), inc="acc")
                    ti += 1
        em.wait("tensor", "acc")
        em.wait("tensor", "q1")
        for j in range(3):
            em.op("tensor", lambda e, j=j: e.matmul(
                p_dot[0:1, j:j + 1], accs[:, j:j + 1], onesb[:, 0:1],
                start=True, stop=True), inc=("par" if j == 2 else None))
        em.wait("vector", "par")
        em.op("vector", lambda e: e.tensor_copy(out=rowp[0:1, 0:3],
                                                in_=p_dot[0:1, 0:3]), inc="par")
        em.wait("gpsimd", "par")
        em.op("gpsimd", lambda e: e.dma_start(out=cc_in[:, :], in_=rowp[:, :]),
              inc="ccs", k=16, dma=True)
        em.wait("gpsimd", "ccs")
        em.op("gpsimd", lambda e: e.collective_compute(
            "AllReduce", ALU.add, replica_groups=[list(range(NCORES))],
            ins=[cc_in.opt()], outs=[cc_out.opt()]), inc="ccv", dma=True)
        em.wait("gpsimd", "ccv")
        em.op("gpsimd", lambda e: e.dma_start(out=srow[:, :], in_=cc_out[:, :]),
              inc="ccs", k=16, dma=True)
        em.wait("vector", "ccs", 32)
        em.wait("vector", "ccv", 1)
        em.op("vector", lambda e: e.tensor_scalar(
            out=srow[0:1, :], in0=srow[0:1, :], scalar1=1.0 / NW,
            scalar2=LN_EPS, op0=ALU.mult, op1=ALU.max))
        em.op("vector", lambda e: e.reciprocal(
            out=rowp[0:1, :], in_=srow[0:1, :]), inc="srow")
        em.wait("tensor", "srow")
        em.op("tensor", lambda e: e.matmul(
            p_bc[:, :], onesc[0:1, :], rowp[0:1, :], start=True, stop=True),
            inc="bc")
        em.wait("vector", "bc")
        em.op("vector", lambda e: e.tensor_copy(out=rsbc[:, :], in_=p_bc[:, :]),
              inc="bc")
        em.wait("sync", "srow")
        em.op("sync", lambda e: e.dma_start(out=sv[:, :], in_=srow[:, :]),
              inc="mout", k=16, dma=True)

        ti2 = 0
        for j, (w_in, w_out, nch, nct) in enumerate(mats):
            for c in range(nch):
                for t in range(nct):
                    par = ti2 % 2
                    if ti2 >= 2:
                        em.wait("gpsimd", "q1", 4 + ti2 - 1)
                    wld.issue("gpsimd", ti + ti2, [
                        lambda e, w_in=w_in, c=c, t=t, par=par: e.dma_start(
                            out=wt[par][:, :],
                            in_=w_in[c * 128:(c + 1) * 128,
                                     t * 2048:(t + 1) * 2048])])
                    em.wait("vector", wld.name(ti + ti2), wld.count(ti + ti2))
                    em.wait("vector", "bc")
                    if ti2 >= 2:
                        em.wait("vector", mst.name(ti2), 16 * (ti2 // 2))
                    em.op("vector", lambda e, par=par, j=j: e.tensor_scalar(
                        out=prodt[:, :], in0=wt[par][:, :],
                        scalar1=rsbc[:, j:j + 1], scalar2=None, op0=ALU.mult))
                    emit_rne(em, lambda: prodt[:, :], lambda: rt_[:, :],
                             2048, i32t, dsc, gsc)
                    em.op("vector", lambda e, par=par: e.tensor_scalar(
                        out=mt[par][:, :], in0=rt_[:, :], scalar1=-1.0,
                        scalar2=1.0, op0=ALU.max, op1=ALU.min), inc="q1")
                    em.wait("sync", "q1", 4 + ti2 + 1)
                    mst.issue("sync", ti2, [
                        lambda e, w_out=w_out, c=c, t=t, par=par: e.dma_start(
                            out=w_out[c * 128:(c + 1) * 128,
                                      t * 2048:(t + 1) * 2048],
                            in_=mt[par][:, :])])
                    ti2 += 1
        em.barrier(["gpsimd"], [mst.names[0], mst.names[1], "mout"])
        em.flush(block)
    return nc



def build_ffn1():
    nc = bass.Bass("TRN2", target_bir_lowering=False, detect_race_conditions=True)
    xTs = nc.dram_tensor("xTs", [D, TPC], F32, kind="ExternalInput").ap()
    mgT = nc.dram_tensor("mgT", [D, 8192], F32, kind="ExternalInput").ap()
    muT = nc.dram_tensor("muT", [D, 8192], F32, kind="ExternalInput").ap()
    mdT = nc.dram_tensor("mdT", [8192, D], F32, kind="ExternalInput").ap()
    svin = nc.dram_tensor("svin", [1, 4], F32, kind="ExternalInput").ap()
    h128 = nc.dram_tensor("h128", [128, 128], F32, kind="ExternalInput").ap()
    identT = nc.dram_tensor("identT", [128, 128], F32, kind="ExternalInput").ap()
    outT = nc.dram_tensor("outT", [D, TPC], F32, kind="ExternalOutput").ap()
    z_i = nc.dram_tensor("z_i", [64, 128, TPC], F32, kind="Internal").ap()

    SL = 256
    NSL = TPC // SL
    R8 = float(np.sqrt(8192.0))
    R2 = float(np.sqrt(2048.0))

    with ExitStack() as ctx:
        sb = lambda n, s, d: ctx.enter_context(nc.sbuf_tensor(n, s, d))
        ps = lambda n, s, d=F32: ctx.enter_context(nc.psum_tensor(n, s, d))
        em = Em(nc, ctx)
        em.sem("gld", "sld", "lni", "f2m", "f2e", "bfl", "h8m", "h8e",
               "amx", "atr", "arow", "abc", "f2q", "gmm", "gev",
               "qmx", "qtr", "qrow", "qbc", "qnt", "dmm", "dev")
        x3 = sb("x3", [128, 16 * TPC], F32)
        h_s = sb("h_s", [128, 128], BF16)
        idf_s = sb("idf_s", [128, 128], F32)
        srow_s = sb("srow_s", [1, 4], F32)
        hi_s = sb("hi_s", [128, 512], BF16)
        lo_s = sb("lo_s", [128, 512], BF16)
        M_s = sb("M_s", [128, 1024], F32)
        amr = sb("amr", [1, TPC], F32)
        brd = sb("brd", [128, SL], F32)
        brdq = sb("brdq", [128, SL], F32)
        i32b = sb("i32b", [128, 512], I32)
        qsc = sb("qsc", [128, 512], F32)
        qsc2 = sb("qsc2", [128, 512], F32)
        wdn = [sb(f"wdn{i}", [128, 1024], BF16) for i in range(2)]
        mtile = sb("mtile", [128, 2], F32)
        lnst = alloc_ln(nc, ctx, TPC)
        mrow, rrow = lnst["mrow"], lnst["rrow"]

        rstk1 = ExitStack()
        xq2 = rstk1.enter_context(
            nc.sbuf_tensor("xq2", [128, 16 * TPC], BF16, side="right"))
        rstk2 = ExitStack()
        fcube = rstk2.enter_context(
            nc.sbuf_tensor("fcube", [128, 16 * 512], F32, side="right"))

        pgrp = ExitStack()
        p_g = pgrp.enter_context(nc.psum_tensor("p_g", [128, 512], F32))
        p_u = pgrp.enter_context(nc.psum_tensor("p_u", [128, 512], F32))
        p_f = [pgrp.enter_context(nc.psum_tensor(f"p_f{i}", [128, 512], F32))
               for i in range(2)]
        p_t = pgrp.enter_context(nc.psum_tensor("p_t", [128, 512], F32))
        block = ctx.enter_context(nc.Block())

        em.op("sync", lambda e: e.dma_start(
            out=x3[:, :].rearrange("p (c t) -> p c t", t=TPC),
            in_=xTs.rearrange("(c p) t -> p c t", p=128)), inc="sld", k=16)
        em.op("sync", lambda e: e.dma_start(out=idf_s[:, :], in_=identT[:, :]),
              inc="sld", k=16)
        em.op("sync", lambda e: e.dma_start(out=srow_s[:, :], in_=svin[:, :]),
              inc="sld", k=16)
        em.op("gpsimd", lambda e: e.dma_start(out=h_s[:, :], in_=h128[:, :]),
              inc="gld", k=16, dma=True)
        emit_ln_init(em, lnst)

        def bcast_row(row_fn, width, out_ap_fn, pbank):
            em.wait("tensor", "abc")
            em.op("tensor", lambda e: e.matmul(
                pbank[:, 0:width], lnst["onesc"][0:1, 0:128], row_fn(),
                start=True, stop=True), inc="abc")
            em.wait("vector", "abc")
            em.op("vector", lambda e: e.tensor_copy(
                out=out_ap_fn(), in_=pbank[:, 0:width]), inc="abc")

        em.barrier(["vector"], ["sld", "lni"])
        em.barrier(["tensor"], ["gld", "sld", "lni"])
        for th2 in range(2):
            w0 = th2 * 512
            for c in range(16):
                pp = c % 2
                em.wait("vector", "h8m")
                em.op("vector", lambda e, c=c, w0=w0: e.tensor_copy(
                    out=hi_s[:, :], in_=x3[:, c * TPC + w0: c * TPC + w0 + 512]))
                em.op("vector", lambda e, c=c, w0=w0: e.tensor_tensor(
                    out=lo_s[:, :], in0=x3[:, c * TPC + w0: c * TPC + w0 + 512],
                    in1=hi_s[:, :], op=ALU.subtract), inc="f2m")
                em.wait("tensor", "f2m")
                em.wait("tensor", "f2e", em.val("f2e") - 1)
                em.op("tensor", lambda e, pp=pp: e.matmul(
                    p_f[pp][:, :], h_s[:, :], hi_s[:, :],
                    start=True, stop=False))
                em.op("tensor", lambda e, pp=pp: e.matmul(
                    p_f[pp][:, :], h_s[:, :], lo_s[:, :],
                    start=False, stop=True), inc="h8m")
                em.wait("scalar", "h8m")
                em.op("scalar", lambda e, c=c, pp=pp: e.activation(
                    out=fcube[:, c * 512: (c + 1) * 512], in_=p_f[pp][:, :],
                    func=AF.Copy), inc="f2e")
            em.barrier(["vector"], ["f2e"])
            slot = {i: i for i in range(16)}
            fsl = [16, 17]

            def sap2(s):
                if s == 16:
                    return qsc[:, 0:512]
                if s == 17:
                    return qsc2[:, 0:512]
                return fcube[:, s * 512:(s + 1) * 512]

            for st_ in range(4):
                hh = 1 << st_
                for a in range(16):
                    if (a // hh) % 2 == 1:
                        continue
                    b = a + hh
                    pa, pb = slot[a], slot[b]
                    na = fsl.pop(0)
                    em.op("vector", lambda e, pa=pa, pb=pb, na=na:
                          e.tensor_tensor(out=sap2(na), in0=sap2(pa),
                                          in1=sap2(pb), op=ALU.add))
                    em.op("vector", lambda e, pa=pa, pb=pb: e.tensor_tensor(
                        out=sap2(pb), in0=sap2(pa), in1=sap2(pb),
                        op=ALU.subtract), inc="bfl")
                    fsl.append(pa)
                    slot[a], slot[b] = na, pb
            em.op("vector", lambda e, s=slot[0]: e.tensor_scalar(
                out=M_s[:, 0:512], in0=sap2(s), scalar1=-1.0, scalar2=None,
                op0=ALU.mult))
            em.op("vector", lambda e, s=slot[0]: e.tensor_tensor(
                out=M_s[:, 0:512], in0=M_s[:, 0:512], in1=sap2(s),
                op=ALU.max))
            for a in range(1, 16):
                em.op("vector", lambda e, s=slot[a]: e.tensor_tensor(
                    out=M_s[:, 0:512], in0=M_s[:, 0:512], in1=sap2(s),
                    op=ALU.max))
                em.op("vector", lambda e, s=slot[a]: e.scalar_tensor_tensor(
                    out=M_s[:, 0:512], in0=sap2(s), scalar=-1.0,
                    in1=M_s[:, 0:512], op0=ALU.mult, op1=ALU.max))
            em.op("vector", lambda e: e.engine_nop(), inc="amx")
            for tt in range(4):
                em.wait("tensor", "amx")
                em.op("tensor", lambda e, tt=tt: e.transpose(
                    p_t[:, 0:128], M_s[:, tt * 128: tt * 128 + 128],
                    idf_s[:, :]), inc="atr")
                em.wait("vector", "atr", em.val("atr"))
                em.op("vector", lambda e: e.tensor_reduce(
                    out=mtile[:, 0:1], in_=p_t[:, 0:128], axis=AX,
                    op=ALU.max), inc="arow")
                em.wait("tensor", "arow", em.val("arow"))
                em.op("tensor", lambda e: e.transpose(
                    p_t[0:1, 128:256], mtile[:, 0:1], idf_s[:, :]), inc="atr")
                em.wait("vector", "atr", em.val("atr"))
                em.op("vector", lambda e, tt=tt, w0=w0: e.tensor_copy(
                    out=amr[0:1, w0 + tt * 128: w0 + tt * 128 + 128],
                    in_=p_t[0:1, 128:256]), inc="arow")
            em.op("vector", lambda e, w0=w0: e.tensor_scalar(
                out=amr[0:1, w0:w0 + 512], in0=amr[0:1, w0:w0 + 512],
                scalar1=R2 * LN_EPS, scalar2=None, op0=ALU.max))
            em.op("vector", lambda e, w0=w0: e.reciprocal(
                out=mrow[0:1, w0:w0 + 512], in_=amr[0:1, w0:w0 + 512]))
            em.op("vector", lambda e, w0=w0: e.tensor_scalar(
                out=mrow[0:1, w0:w0 + 512], in0=mrow[0:1, w0:w0 + 512],
                scalar1=127.0, scalar2=None, op0=ALU.mult), inc="arow")
            em.wait("tensor", "arow")
            bcast_row(lambda w0=w0: mrow[0:1, w0:w0 + 512], 512,
                      lambda: M_s[:, 0:512], p_t)
            for c in range(16):
                em.wait("vector", "abc")
                em.op("vector", lambda e, s=slot[c]: e.tensor_tensor(
                    out=M_s[:, 512:1024], in0=sap2(s), in1=M_s[:, 0:512],
                    op=ALU.mult))
                emit_rne(em, lambda: M_s[:, 512:1024],
                         lambda c=c, w0=w0: xq2[:, c * TPC + w0:
                                                c * TPC + w0 + 512],
                         512, i32b, lnst["ntmp"], lnst["nout"][0], inc="f2q")
        em.op("vector", lambda e: e.tensor_scalar(
            out=amr[0:1, :], in0=amr[0:1, :], scalar1=1.0 / (R2 * 127.0),
            scalar2=None, op0=ALU.mult))
        em.op("vector", lambda e: e.tensor_scalar(
            out=mrow[0:1, 0:TPC], in0=amr[0:1, :],
            scalar1=srow_s[0:1, 0:1], scalar2=None, op0=ALU.mult))
        em.op("vector", lambda e: e.tensor_scalar(
            out=rrow[0:1, 0:TPC], in0=amr[0:1, :],
            scalar1=srow_s[0:1, 1:2], scalar2=None, op0=ALU.mult), inc="arow")
        em.wait("tensor", "arow")
        rstk2.close()

        gstk = ExitStack()
        brg = gstk.enter_context(nc.sbuf_tensor("brg", [128, TPC], F32))
        bru = gstk.enter_context(nc.sbuf_tensor("bru", [128, TPC], F32))
        gtmp = gstk.enter_context(nc.sbuf_tensor("gtmp", [128, 512], F32))
        utmp = gstk.enter_context(nc.sbuf_tensor("utmp", [128, 512], F32))
        ztmp = [gstk.enter_context(nc.sbuf_tensor(f"ztmp{i}", [128, 512], F32))
                for i in range(2)]
        wgu = [gstk.enter_context(
            nc.sbuf_tensor(f"wgu{i}", [128, 16 * 256], BF16)) for i in range(2)]
        zwr = Dma2(em, "zwr")
        wgl = Dma2(em, "wgl")
        em.wait("gpsimd", "f2q")
        em.wait("vector", "f2q")
        for hf in range(2):
            bcast_row(lambda hf=hf: mrow[0:1, hf * 512: hf * 512 + 512], 512,
                      lambda hf=hf: brg[:, hf * 512: hf * 512 + 512], p_t)
            bcast_row(lambda hf=hf: rrow[0:1, hf * 512: hf * 512 + 512], 512,
                      lambda hf=hf: bru[:, hf * 512: hf * 512 + 512], p_t)
        gi = 0
        for oc in range(64):
            par = oc % 2
            if oc >= 2:
                em.wait("gpsimd", "gmm", 2 * (oc - 1))
            wgc = wgl.issue("gpsimd", oc, [
                lambda e, oc=oc, par=par: e.dma_start(
                    out=wgu[par][:, :].rearrange("p (c o) -> p c o", o=256)
                    [:, :, 0:128],
                    in_=mgT.rearrange("(c p) o -> p c o", p=128)
                    [:, :, oc * 128: oc * 128 + 128]),
                lambda e, oc=oc, par=par: e.dma_start(
                    out=wgu[par][:, :].rearrange("p (c o) -> p c o", o=256)
                    [:, :, 128:256],
                    in_=muT.rearrange("(c p) o -> p c o", p=128)
                    [:, :, oc * 128: oc * 128 + 128])])
            for th in range(2):
                u = gi
                gi += 1
                em.wait("tensor", wgl.name(oc), wgc)
                em.wait("tensor", "f2q")
                if u >= 1:
                    em.wait("tensor", "gev", 3 * u)
                for kc in range(16):
                    em.op("tensor", lambda e, par=par, kc=kc, th=th: e.matmul(
                        p_g[:, :], wgu[par][:, kc * 256: kc * 256 + 128],
                        xq2[:, kc * TPC + th * 512: kc * TPC + th * 512 + 512],
                        start=(kc == 0), stop=(kc == 15)))
                for kc in range(16):
                    em.op("tensor", lambda e, par=par, kc=kc, th=th: e.matmul(
                        p_u[:, :], wgu[par][:, kc * 256 + 128: kc * 256 + 256],
                        xq2[:, kc * TPC + th * 512: kc * TPC + th * 512 + 512],
                        start=(kc == 0), stop=(kc == 15)),
                        inc=("gmm" if kc == 15 else None))
                em.wait("vector", "gmm", u + 1)
                if u >= 1:
                    em.wait("vector", "gev", 3 * (u - 1) + 2)
                em.op("vector", lambda e, th=th: e.tensor_tensor(
                    out=gtmp[:, :], in0=p_g[:, :],
                    in1=brg[:, th * 512: th * 512 + 512], op=ALU.mult),
                    inc="gev")
                em.wait("scalar", "gev", 3 * u + 1)
                em.op("scalar", lambda e: e.activation(
                    out=gtmp[:, :], in_=gtmp[:, :], func=AF.Silu), inc="gev")
                em.wait("vector", "gev", 3 * u + 2)
                if u >= 2:
                    em.wait("vector", zwr.name(u), 16 * (u // 2))
                em.op("vector", lambda e, th=th: e.tensor_tensor(
                    out=utmp[:, :], in0=p_u[:, :],
                    in1=bru[:, th * 512: th * 512 + 512], op=ALU.mult))
                em.op("vector", lambda e, u=u: e.tensor_tensor(
                    out=ztmp[u % 2][:, :], in0=gtmp[:, :], in1=utmp[:, :],
                    op=ALU.mult), inc="gev")
                em.wait("sync", "gev", 3 * u + 3)
                zwr.issue("sync", u, [
                    lambda e, oc=oc, th=th, u=u: e.dma_start(
                        out=z_i[oc, :, th * 512: th * 512 + 512],
                        in_=ztmp[u % 2][:, :])])
        em.barrier(list(ENGS), [zwr.names[0], zwr.names[1]])

        gstk.close()
        rstk1.close()
        pgrp.close()
        fc2 = ExitStack()
        fcube2 = fc2.enter_context(
            nc.sbuf_tensor("fcube2", [128, 64 * SL], F32, side="right"))
        xq8c = fc2.enter_context(
            nc.sbuf_tensor("xq8c", [128, 64 * SL], BF16, side="right"))
        p_dn = [fc2.enter_context(nc.psum_tensor(f"p_dn{i}", [128, 512], F32))
                for i in range(8)]

        zld = Dma2(em, "zld")
        dwl = Dma2(em, "dwl")
        for sl in range(NSL):
            t0 = sl * SL
            if sl >= 1:
                em.wait("gpsimd", "qnt", 64 * sl)
            zcnt = zld.issue("gpsimd", sl, [
                lambda e, t0=t0: e.dma_start(
                    out=fcube2[:, :].rearrange("p (c t) -> p c t", t=SL),
                    in_=z_i[:, :, t0:t0 + SL].rearrange("c p t -> p c t"))])
            em.wait("vector", zld.name(sl), zcnt)
            slot8 = {i: i for i in range(64)}
            fsl8 = [64, 65]

            def sap(s):
                if s == 64:
                    return qsc[:, 0:SL]
                if s == 65:
                    return qsc2[:, 0:SL]
                return fcube2[:, s * SL:(s + 1) * SL]

            for st_ in range(6):
                hh = 1 << st_
                for a in range(64):
                    if (a // hh) % 2 == 1:
                        continue
                    b = a + hh
                    pa, pb = slot8[a], slot8[b]
                    na = fsl8.pop(0)
                    em.op("vector", lambda e, pa=pa, pb=pb, na=na:
                          e.tensor_tensor(out=sap(na), in0=sap(pa),
                                          in1=sap(pb), op=ALU.add))
                    em.op("vector", lambda e, pa=pa, pb=pb: e.tensor_tensor(
                        out=sap(pb), in0=sap(pa), in1=sap(pb),
                        op=ALU.subtract), inc="bfl")
                    fsl8.append(pa)
                    slot8[a], slot8[b] = na, pb
            if sl >= 1:
                em.wait("tensor", "dev", 16 * sl)
            for c in range(64):
                pp = c % 2
                em.wait("vector", "h8m")
                em.op("vector", lambda e, s=slot8[c]: e.tensor_copy(
                    out=hi_s[:, 0:SL], in_=sap(s)))
                em.op("vector", lambda e, s=slot8[c]: e.tensor_tensor(
                    out=lo_s[:, 0:SL], in0=sap(s), in1=hi_s[:, 0:SL],
                    op=ALU.subtract), inc="f2m")
                em.wait("tensor", "f2m")
                em.wait("tensor", "h8e", em.val("h8e") - 1)
                em.op("tensor", lambda e, pp=pp: e.matmul(
                    p_dn[pp][:, 0:SL], h_s[:, :], hi_s[:, 0:SL],
                    start=True, stop=False))
                em.op("tensor", lambda e, pp=pp: e.matmul(
                    p_dn[pp][:, 0:SL], h_s[:, :], lo_s[:, 0:SL],
                    start=False, stop=True), inc="h8m")
                em.wait("scalar", "h8m")
                em.op("scalar", lambda e, s=slot8[c], pp=pp: e.activation(
                    out=sap(s), in_=p_dn[pp][:, 0:SL], func=AF.Copy),
                    inc="h8e")
                em.wait("vector", "h8e")
                if c == 0:
                    em.op("vector", lambda e, s=slot8[c]: e.tensor_scalar(
                        out=M_s[:, 0:SL], in0=sap(s), scalar1=-1.0,
                        scalar2=None, op0=ALU.mult))
                    em.op("vector", lambda e, s=slot8[c]: e.tensor_tensor(
                        out=M_s[:, 0:SL], in0=M_s[:, 0:SL], in1=sap(s),
                        op=ALU.max), inc="qmx")
                else:
                    em.op("vector", lambda e, s=slot8[c]: e.tensor_tensor(
                        out=M_s[:, 0:SL], in0=M_s[:, 0:SL], in1=sap(s),
                        op=ALU.max))
                    em.op("vector", lambda e, s=slot8[c]: e.scalar_tensor_tensor(
                        out=M_s[:, 0:SL], in0=sap(s), scalar=-1.0,
                        in1=M_s[:, 0:SL], op0=ALU.mult, op1=ALU.max),
                        inc="qmx")
            for tt in range(SL // 128):
                em.wait("tensor", "qmx")
                em.op("tensor", lambda e, tt=tt: e.transpose(
                    p_dn[0][:, 0:128], M_s[:, tt * 128: tt * 128 + 128],
                    idf_s[:, :]), inc="qtr")
                em.wait("vector", "qtr", em.val("qtr"))
                em.op("vector", lambda e: e.tensor_reduce(
                    out=mtile[:, 0:1], in_=p_dn[0][:, 0:128], axis=AX,
                    op=ALU.max), inc="qrow")
                em.wait("tensor", "qrow", em.val("qrow"))
                em.op("tensor", lambda e: e.transpose(
                    p_dn[0][0:1, 128:256], mtile[:, 0:1], idf_s[:, :]),
                    inc="qtr")
                em.wait("vector", "qtr", em.val("qtr"))
                em.op("vector", lambda e, tt=tt: e.tensor_copy(
                    out=amr[0:1, tt * 128: tt * 128 + 128],
                    in_=p_dn[0][0:1, 128:256]), inc="qrow")
            em.op("vector", lambda e: e.tensor_scalar(
                out=amr[0:1, 0:SL], in0=amr[0:1, 0:SL], scalar1=R8 * LN_EPS,
                scalar2=None, op0=ALU.max))
            em.op("vector", lambda e: e.reciprocal(
                out=mrow[0:1, 0:SL], in_=amr[0:1, 0:SL]))
            em.op("vector", lambda e: e.tensor_scalar(
                out=mrow[0:1, 0:SL], in0=mrow[0:1, 0:SL], scalar1=127.0,
                scalar2=None, op0=ALU.mult), inc="qrow")
            em.wait("tensor", "qrow")
            bcast_row(lambda: mrow[0:1, 0:SL], SL, lambda: brdq[:, 0:SL],
                      p_dn[0])
            em.op("vector", lambda e: e.tensor_scalar(
                out=amr[0:1, 0:SL], in0=amr[0:1, 0:SL],
                scalar1=1.0 / (R8 * 127.0), scalar2=None, op0=ALU.mult))
            em.op("vector", lambda e: e.tensor_scalar(
                out=amr[0:1, 0:SL], in0=amr[0:1, 0:SL],
                scalar1=srow_s[0:1, 2:3], scalar2=None, op0=ALU.mult),
                inc="qrow")
            em.wait("tensor", "qrow")
            bcast_row(lambda: amr[0:1, 0:SL], SL, lambda: brd[:, 0:SL],
                      p_dn[0])
            em.wait("vector", "abc")
            for c in range(64):
                em.op("vector", lambda e, s=slot8[c]: e.tensor_tensor(
                    out=M_s[:, 512:512 + SL], in0=sap(s), in1=brdq[:, 0:SL],
                    op=ALU.mult))
                emit_rne(em, lambda: M_s[:, 512:512 + SL],
                         lambda c=c: xq8c[:, c * SL:(c + 1) * SL],
                         SL, i32b, lnst["ntmp"], lnst["nout"][0], inc="qnt")
            for half in range(2):
                if half == 0:
                    em.wait("tensor", "abc")
                for k in range(64):
                    gk = (sl * 2 + half) * 64 + k
                    if gk >= 2:
                        em.wait("gpsimd", "dmm", gk - 1)
                    dwc = dwl.issue("gpsimd", gk, [
                        lambda e, k=k, half=half: e.dma_start(
                            out=wdn[k % 2][:, :],
                            in_=mdT[k * 128:(k + 1) * 128,
                                    half * 1024:(half + 1) * 1024])])
                    em.wait("tensor", "qnt", 64 * (sl + 1))
                    em.wait("tensor", dwl.name(gk), dwc)
                    for oc8 in range(8):
                        em.op("tensor", lambda e, k=k, oc8=oc8: e.matmul(
                            p_dn[oc8][:, 0:SL],
                            wdn[k % 2][:, oc8 * 128: oc8 * 128 + 128],
                            xq8c[:, k * SL:(k + 1) * SL],
                            start=(k == 0), stop=(k == 63)),
                            inc=("dmm" if oc8 == 7 else None))
                em.wait("vector", "dmm")
                for oc8 in range(8):
                    oc = half * 8 + oc8
                    em.op("vector", lambda e, oc8=oc8: e.tensor_tensor(
                        out=qsc[:, 0:SL], in0=p_dn[oc8][:, 0:SL],
                        in1=brd[:, 0:SL], op=ALU.mult))
                    em.op("vector", lambda e, oc=oc, t0=t0: e.tensor_tensor(
                        out=x3[:, oc * TPC + t0: oc * TPC + t0 + SL],
                        in0=qsc[:, 0:SL],
                        in1=x3[:, oc * TPC + t0: oc * TPC + t0 + SL],
                        op=ALU.add), inc="dev")
                em.wait("tensor", "dev")
        em.barrier(list(ENGS), ["dev"])

        fc2.close()
        p_A = ps("p_A", [128, 512])
        p_B = ps("p_B", [128, 512])
        p_S = [ps("p_S0", [128, 1]), ps("p_S1", [128, 1])]
        p_T = ps("p_T2", [1, 256])
        wrn = emit_ln(em, nc, x3, lnst, p_A, p_B, p_S, p_T, idf_s, outT,
                      TPC, "L")
        em.barrier(["gpsimd"], wrn)
        em.flush(block)
    return nc



_COS, _SIN = None, None
_NC_CACHE = {}


def _consts():
    global _COS, _SIN
    if _COS is None:
        _COS, _SIN = _rope_tables()
    return _COS, _SIN


def _nc(key, builder, *args):
    if key not in _NC_CACHE:
        _NC_CACHE[key] = builder(*args)
    return _NC_CACHE[key]


_IDENT = np.eye(128, dtype=np.float32)


def _run_attn(layer, xT_full, wq, wk, wv):
    cos, sin = _consts()
    plan = _PLAN0 if layer == 0 else _PLAN1
    rt = _r_interleaved() if layer == 0 else _r_halfswap()
    nc = _nc(("attn", layer), build_attn, plan)
    in_maps = []
    for c in range(NCORES):
        rows = slice(256 * c, 256 * c + 256)
        in_maps.append({
            "xT": xT_full,
            "wqT": np.ascontiguousarray(wq[rows, :].T),
            "wkT": np.ascontiguousarray(wk[rows, :].T),
            "wvT": np.ascontiguousarray(wv[rows, :].T),
            "cosT": cos, "sinT": sin, "rT": rt,
            "identT": _IDENT, "maskc": _MASKS,
        })
    res = run_bass_kernel_spmd(nc, in_maps, core_ids=list(range(NCORES)))
    return np.concatenate(
        [np.asarray(r["ctxT"]).reshape(256, T) for r in res.results], axis=0)


def _run_proj_ln(ctxT_full, wo, resT_full):
    nc = _nc("proj", build_proj_ln)
    woT = np.ascontiguousarray(wo.T)
    in_maps = []
    for c in range(NCORES):
        cols = slice(TPC * c, TPC * c + TPC)
        in_maps.append({
            "ctxTs": np.ascontiguousarray(ctxT_full[:, cols]),
            "woT": woT,
            "resT": np.ascontiguousarray(resT_full[:, cols]),
            "identT": _IDENT,
        })
    res = run_bass_kernel_spmd(nc, in_maps, core_ids=list(range(NCORES)))
    return np.concatenate([r["outT"] for r in res.results], axis=1)


def _run_ffn0(xT_full, W1, b1, W2, b2, U, V):
    nc = _nc("ffn0", build_ffn0)
    w1T = np.ascontiguousarray(W1)
    vfT = np.ascontiguousarray(V.reshape(512, D).T)
    uf = np.ascontiguousarray(U.transpose(0, 2, 1).reshape(512, D))
    in_maps = []
    for c in range(NCORES):
        cols = slice(TPC * c, TPC * c + TPC)
        in_maps.append({
            "xTs": np.ascontiguousarray(xT_full[:, cols]),
            "w1T": w1T, "b1": b1, "w2": W2,
            "b2": b2.reshape(1, 8), "vfT": vfT, "uf": uf,
            "identT": _IDENT,
        })
    res = run_bass_kernel_spmd(nc, in_maps, core_ids=list(range(NCORES)))
    return np.concatenate([r["outT"] for r in res.results], axis=1)


def _run_attn_l1(xT_full, wq, wk, wv):
    return _run_attn(1, xT_full, wq, wk, wv)


def _run_wquant(w_gate, w_up, w_down):
    nc = _nc("wq", build_wquant)
    wgT = np.ascontiguousarray(w_gate.T)
    wuT = np.ascontiguousarray(w_up.T)
    wdT = np.ascontiguousarray(w_down.T)
    in_maps = []
    for c in range(NCORES):
        in_maps.append({
            "wg": np.ascontiguousarray(wgT[256 * c: 256 * c + 256]),
            "wu": np.ascontiguousarray(wuT[256 * c: 256 * c + 256]),
            "wd": np.ascontiguousarray(wdT[1024 * c: 1024 * c + 1024]),
        })
    res = run_bass_kernel_spmd(nc, in_maps, core_ids=list(range(NCORES)))
    mgT = np.concatenate([r["mg"] for r in res.results], axis=0)
    muT = np.concatenate([r["mu"] for r in res.results], axis=0)
    mdT = np.concatenate([r["md"] for r in res.results], axis=0)
    return mgT, muT, mdT, np.asarray(res.results[0]["sv"])


def _run_ffn1(xT_full, mgT, muT, mdT, sv):
    nc = _nc("ffn1", build_ffn1)
    h128c = _hadamard(128)
    in_maps = []
    for c in range(NCORES):
        cols = slice(TPC * c, TPC * c + TPC)
        in_maps.append({
            "xTs": np.ascontiguousarray(xT_full[:, cols]),
            "mgT": mgT, "muT": muT, "mdT": mdT, "svin": sv,
            "h128": h128c, "identT": _IDENT,
        })
    res = run_bass_kernel_spmd(nc, in_maps, core_ids=list(range(NCORES)))
    return np.concatenate([r["outT"] for r in res.results], axis=1)


def kernel(**inputs):
    x0 = np.asarray(inputs["L0_attn_x"])[0]
    x0T = np.ascontiguousarray(x0.T)
    Wqkv = np.asarray(inputs["L0_attn_Wqkv"])
    ctxT = _run_attn(0, x0T, Wqkv[0:D], Wqkv[D:2 * D], Wqkv[2 * D:3 * D])
    x1T = _run_proj_ln(ctxT, np.asarray(inputs["L0_attn_Wout"]), x0T)
    x2T = _run_ffn0(x1T, np.asarray(inputs["L0_ffn_W1"]),
                    np.asarray(inputs["L0_ffn_b1"]),
                    np.asarray(inputs["L0_ffn_W2"]),
                    np.asarray(inputs["L0_ffn_b2"]),
                    np.asarray(inputs["L0_ffn_U"]),
                    np.asarray(inputs["L0_ffn_V"]))
    ctxT2 = _run_attn(1, x2T, np.asarray(inputs["L1_attn_w_q"]),
                      np.asarray(inputs["L1_attn_w_k"]),
                      np.asarray(inputs["L1_attn_w_v"]))
    x3T = _run_proj_ln(ctxT2, np.asarray(inputs["L1_attn_w_o"]), x2T)
    mgT, muT, mdT, sv = _run_wquant(np.asarray(inputs["L1_ffn_w_gate"]),
                                    np.asarray(inputs["L1_ffn_w_up"]),
                                    np.asarray(inputs["L1_ffn_w_down"]))
    yT = _run_ffn1(x3T, mgT, muT, mdT, sv)
    return np.ascontiguousarray(yT.T)[None, :, :].astype(np.float32)
